# version 51
# baseline (speedup 1.0000x reference)
import dataclasses
import contextlib
import numpy as np
import ml_dtypes
import concourse.bass as bass
import concourse.mybir as mybir
from concourse.bass_utils import run_bass_kernel_spmd

F32 = mybir.dt.float32
BF16 = mybir.dt.bfloat16
I32 = mybir.dt.int32
AF = mybir.ActivationFunctionType
OP = mybir.AluOpType

B, L, DM, DI, NST, RNK, HID = 2, 8192, 1024, 2048, 16, 64, 3072
P = 128
TOK = L // 4
DQ = DI // 4
S127 = 1.0 / 127.0
EPS = 1e-5
KD = DM // P
MH = HID // P

BF = ml_dtypes.bfloat16


def rev(ap):
    newap = [list(d) for d in ap.ap]
    step, cnt = newap[-1]
    assert step == 1
    return dataclasses.replace(ap, offset=ap.offset + cnt - 1,
                               ap=newap[:-1] + [[-1, cnt]])


def bcast_free(ap, n, axis_len):
    newap = [list(d) for d in ap.ap]
    assert newap[-1][1] == axis_len
    return dataclasses.replace(ap, ap=newap[:-1] + [[0, n], list(newap[-1])])



class Rec:
    __slots__ = ("fn", "incs")

    def __init__(self, fn):
        self.fn = fn
        self.incs = []


class Q:
    def __init__(self, name):
        self.name = name
        self.q = []

    def op(self, method, *a, **kw):
        r = Rec(lambda eng: getattr(eng, method)(*a, **kw))
        self.q.append(r)
        return r

    def wait(self, sem, n=None):
        v = sem.n if n is None else n
        self.q.append(("wait", sem.h, v))

    def flush(self, eng):
        for item in self.q:
            if isinstance(item, Rec):
                instr = item.fn(eng)
                for (s, k) in item.incs:
                    instr.then_inc(s, k)
            else:
                _, h, v = item
                if v > 0:
                    eng.wait_ge(h, v)


class Sem:
    def __init__(self, handle):
        self.h = handle
        self.n = 0

    def inc(self, rec, k=1):
        rec.incs.append((self.h, k))
        self.n += k
        return rec


class Ctx:
    def __init__(self, nc, es):
        self.nc = nc
        self.es = es
        self.sync = Q("sync")
        self.tensor = Q("tensor")
        self.scalar = Q("scalar")
        self.vector = Q("vector")
        self.gpsimd = Q("gpsimd")
        self.sems = {}

    def sem(self, name):
        if name not in self.sems:
            self.sems[name] = Sem(self.es.enter_context(self.nc.semaphore(name)))
        return self.sems[name]

    def flush(self):
        block = self.es.enter_context(self.nc.Block())
        for name in ("sync", "tensor", "scalar", "vector", "gpsimd"):
            q = getattr(self, name)
            if q.q:
                getattr(block, name)(lambda eng, q=q: q.flush(eng))

    def sb(self, name, shape, dtype):
        return self.es.enter_context(self.nc.sbuf_tensor(name, shape, dtype))

    def ps(self, name, shape, dtype=F32):
        return self.es.enter_context(self.nc.psum_tensor(name, shape, dtype))



def kstack_src(dram_ap, row0, nk, colw, col0):
    C = dram_ap.ap[-2][0] if len(dram_ap.ap) >= 2 else None
    C = dram_ap.ap[0][0]
    return dataclasses.replace(
        dram_ap, offset=dram_ap.offset + row0 * C + col0,
        ap=[[C, P], [P * C, nk], [1, colw]])


def kstack_dst(tile_ap, nk, colw):
    pitch = tile_ap.ap[0][0]
    return dataclasses.replace(
        tile_ap, ap=[[pitch, P], [colw, nk], [1, colw]])



def emit_weight_convert(cx, tiles, wints, wbfs, s_wld2, s_cast, s_wst2):
    for i, (src, dst, k, width) in enumerate(tiles):
        pb = i % 2
        cnt = i // 2
        stage = wints[pb][:, :width]
        if src.dtype == F32:
            stage = stage.bitcast(F32)
        cx.sync.wait(s_cast, max(0, s_cast.n - 1))
        s_wld2[pb].inc(cx.sync.op("dma_start", out=stage,
                                  in_=src[k * P:(k + 1) * P, :]), 16)
        cx.vector.wait(s_wld2[pb], 16 * (cnt + 1))
        cx.vector.wait(s_wst2[pb], 16 * cnt)
        s_cast.inc(cx.vector.op("tensor_copy", wbfs[pb][:, :width], stage))
        cx.gpsimd.wait(s_cast)
        s_wst2[pb].inc(cx.gpsimd.op(
            "dma_start", out=dst[k * P:(k + 1) * P, :],
            in_=wbfs[pb][:, :width]), 16)


def emit_ln_stats(cx, src_bf, sq_bf, nk, tc, consts, pst, rows, bcst, s, dim):
    ones_col, ones_row, eps_ap = consts
    ps_s, ps_q, ps_b = pst
    cx.tensor.wait(s["act"])
    cx.tensor.wait(s["dve"])
    cx.tensor.wait(s["actb"])
    for k in range(nk):
        cx.tensor.op("matmul", ps_s[:, :], ones_col[0:P, 0:1],
                     src_bf[:, k * tc:(k + 1) * tc],
                     start=(k == 0), stop=(k == nk - 1))
    for k in range(nk):
        r = cx.tensor.op("matmul", ps_q[:, :], ones_col[0:P, 0:1],
                         sq_bf[:, k * tc:(k + 1) * tc],
                         start=(k == 0), stop=(k == nk - 1))
    s["pe"].inc(r)
    mu = rows[:, 0:tc]
    msq = rows[:, tc:2 * tc]
    var = rows[:, 2 * tc:3 * tc]
    rstd = rows[:, 3 * tc:4 * tc]
    cx.scalar.wait(s["pe"])
    cx.scalar.wait(s["dveb"])
    s["act"].inc(cx.scalar.op("activation", mu, ps_s[:, :], AF.Copy,
                              scale=1.0 / dim))
    s["act"].inc(cx.scalar.op("activation", msq, ps_q[:, :], AF.Copy, scale=1.0 / dim))
    cx.vector.wait(s["act"])
    s["dveb"].inc(cx.vector.op("tensor_mul", var, mu, mu))
    cx.vector.wait(s["dveb"], s["dveb"].n)
    s["dveb"].inc(cx.vector.op("tensor_sub", var, msq, var))
    cx.scalar.wait(s["dveb"])
    s["act"].inc(cx.scalar.op("activation", rstd, var, AF.Ln, bias=eps_ap))
    cx.scalar.wait(s["act"], s["act"].n)
    s["act"].inc(cx.scalar.op("activation", rstd, rstd, AF.Exp, scale=-0.5))
    cx.vector.wait(s["act"])
    s["dveb"].inc(cx.vector.op("tensor_mul", mu, mu, rstd))
    cx.tensor.wait(s["dveb"])
    cx.tensor.wait(s["act"])
    cx.tensor.op("matmul", ps_b[:, 0:tc], ones_row[0:1, 0:P], rstd,
                 start=True, stop=True)
    s["pe"].inc(cx.tensor.op("matmul", ps_b[:, tc:2 * tc], ones_row[0:1, 0:P], mu,
                             start=True, stop=True))
    cx.scalar.wait(s["pe"])
    cx.scalar.wait(s["dvec"])
    s["act"].inc(cx.scalar.op("activation", bcst[:, 0:tc],
                              ps_b[:, 0:tc], AF.Copy))
    s["act"].inc(cx.scalar.op("activation", bcst[:, tc:2 * tc],
                              ps_b[:, tc:2 * tc], AF.Copy))



def emit_ffn(cx, vln, hb, xres, xout, wg_s, wu_s, wd_s, wmm, wdm, g_t, u_t,
             ps_m, s, TC, out_scale, wst_target=None):
    if wst_target is None:
        cx.sync.wait(s["wst"], s["wst"].n)
        cx.sync.wait(s["wstb"], s["wstb"].n)
    else:
        nt_ = wst_target // 16
        cx.sync.wait(s["wst"], 16 * ((nt_ + 1) // 2))
        cx.sync.wait(s["wstb"], 16 * (nt_ // 2))

    """SwiGLU FFN on one token chunk: xout = xres + out_scale*(FFN(vln)).
    vln: [P, KD*TC] bf16; hb: [P, MH*TC] bf16 scratch; xres/xout [P, KD*TC] f32."""
    for m in range(MH):
        pb = m % 2
        wsl = wmm[:, (2 * pb) * KD * P:(2 * pb + 1) * KD * P]
        usl = wmm[:, (2 * pb + 1) * KD * P:(2 * pb + 2) * KD * P]
        cx.sync.wait(s["mm"], max(0, s["mm"].n - 1))
        sw = s["w"] if pb == 0 else s["wb"]
        sw.inc(cx.sync.op("dma_start", out=kstack_dst(wsl, KD, P),
                          in_=kstack_src(wg_s, 0, KD, P, m * P)), 16)
        sw.inc(cx.sync.op("dma_start", out=kstack_dst(usl, KD, P),
                          in_=kstack_src(wu_s, 0, KD, P, m * P)), 16)
        cx.tensor.wait(sw, sw.n)
        cx.tensor.wait(s["dvec"])
        cx.tensor.wait(s["evac"], max(0, s["evac"].n - 1))
        for k in range(KD):
            cx.tensor.op("matmul", ps_m[2 * pb][:, :],
                         wsl[:, k * P:(k + 1) * P],
                         vln[:, k * TC:(k + 1) * TC],
                         start=(k == 0), stop=(k == KD - 1))
        for k in range(KD):
            r = cx.tensor.op("matmul", ps_m[2 * pb + 1][:, :],
                             usl[:, k * P:(k + 1) * P],
                             vln[:, k * TC:(k + 1) * TC],
                             start=(k == 0), stop=(k == KD - 1))
        s["mm"].inc(r)
        gsl = g_t[:, pb * TC:(pb + 1) * TC]
        usl2 = u_t[:, pb * TC:(pb + 1) * TC]
        cx.scalar.wait(s["mm"])
        cx.scalar.wait(s["dve"], max(0, s["dve"].n - 1))
        cx.scalar.op("activation", gsl, ps_m[2 * pb][:, :], AF.Silu, scale=S127)
        s["evac"].inc(cx.scalar.op("activation", usl2, ps_m[2 * pb + 1][:, :],
                                   AF.Copy, scale=S127))
        cx.vector.wait(s["evac"])
        s["dve"].inc(cx.vector.op("tensor_mul",
                                  hb[:, m * TC:(m + 1) * TC], gsl, usl2))

    for m in range(KD):
        pb = m % 2
        wsl = wdm[:, pb * MH * P:(pb + 1) * MH * P]
        cx.sync.wait(s["mm"], max(0, s["mm"].n - 1))
        sw = s["w"] if pb == 0 else s["wb"]
        sw.inc(cx.sync.op("dma_start", out=kstack_dst(wsl, MH, P),
                          in_=kstack_src(wd_s, 0, MH, P, m * P)), 16)
        cx.tensor.wait(sw, sw.n)
        cx.tensor.wait(s["dve"])
        cx.tensor.wait(s["evac"], max(0, s["evac"].n - 1))
        for k in range(MH):
            r = cx.tensor.op("matmul", ps_m[pb][:, :],
                             wsl[:, k * P:(k + 1) * P],
                             hb[:, k * TC:(k + 1) * TC],
                             start=(k == 0), stop=(k == MH - 1))
        s["mm"].inc(r)
        cx.vector.wait(s["mm"])
        cx.vector.wait(s["st"])
        r = cx.vector.op("scalar_tensor_tensor",
                         xout[:, m * TC:(m + 1) * TC], ps_m[pb][:, :],
                         out_scale, xres[:, m * TC:(m + 1) * TC],
                         OP.mult, OP.add)
        s["evac"].inc(r)



def emit_p1(nc, cx, io, s, es, pre="", extra_tiles=(), v_store_hook=None):
    TC = 512
    NCH = TOK // TC
    sb = lambda n, sh, dt: es.enter_context(nc.sbuf_tensor(pre + n, sh, dt))
    ps = lambda n, sh: es.enter_context(nc.psum_tensor(pre + n, sh, F32))
    if True:
        xs = sb("xs", [P, KD * TC], F32)
        cb = sb("cb", [P, KD * TC], BF16)
        sqb = sb("sqb", [P, KD * TC], BF16)
        vln = sb("vln", [P, KD * TC], BF16)
        vaf = sb("vaf", [P, KD * TC], BF16)
        tmpf = sb("tmpf", [P, KD * TC], F32)
        x1s = sb("x1s", [P, KD * TC], F32)
        us = sb("us", [P, KD * TC], F32)
        uns = sb("uns", [P, KD * TC], F32)
        hb = sb("hb", [P, MH * TC], BF16)
        wmm = sb("wmm", [P, 4 * KD * P], BF16)
        wdm = sb("wdm", [P, 2 * MH * P], BF16)
        rows = sb("rows", [1, 4 * TC], F32)
        bcst = sb("bcst", [P, 2 * TC], F32)
        colv = sb("colv", [P, KD * 4], F32)
        g_t = sb("g_t", [P, 2 * TC], BF16)
        u_t = sb("u_t", [P, 2 * TC], BF16)

        ps_s = ps("ps_s", [1, TC])
        ps_q = ps("ps_q", [1, TC])
        ps_b = ps("ps_b", [P, 2 * TC])
        ps_m = [ps("ps_m0", [P, TC]), ps("ps_m1", [P, TC]),
                ps("ps_m2", [P, TC]), ps("ps_m3", [P, TC])]


        onc = sb("onc", [P, 1], BF16)
        onr = sb("onr", [1, P], F32)
        epc = sb("epc", [P, 1], F32)
        s["ld"].inc(cx.sync.op("dma_start", out=onc[:, :], in_=io.ones_col), 16)
        s["ld"].inc(cx.sync.op("dma_start", out=onr[:, :], in_=io.ones_row), 16)
        s["ld"].inc(cx.sync.op("dma_start", out=epc[:, :], in_=io.eps_col), 16)
        colsrc = dataclasses.replace(io.cols, ap=[[4, P], [4 * P, KD], [1, 4]])
        s["ld"].inc(cx.sync.op("dma_start", out=colv[:, :], in_=colsrc), 16)
        conv_tiles = (
            [(io.wgT, io.wg_s, k, HID) for k in range(KD)]
            + [(io.wuT, io.wu_s, k, HID) for k in range(KD)]
            + [(io.wdT, io.wd_s, k, DM) for k in range(MH)]
            + list(extra_tiles)
        )
        n_ffn_tiles = 2 * KD + MH
        emit_weight_convert(cx, conv_tiles,
                            [xs[:, :].bitcast(I32), tmpf[:, :].bitcast(I32)],
                            [cb[:, :], sqb[:, :]],
                            [s["wld"], s["wldb"]], s["cast"],
                            [s["wst"], s["wstb"]])
        wst_ffn = 16 * n_ffn_tiles

        def lncol(j):
            return lambda k: colv[:, 4 * k + j:4 * k + j + 1]
        lnw_f, lnb_f, lnw_b, lnb_b = lncol(0), lncol(1), lncol(2), lncol(3)
        consts = (onc[:, :], onr[:, :], epc[0:1, :])
        pst = (ps_s, ps_q, ps_b)

        for c in range(NCH):
            col = slice(c * TC, (c + 1) * TC)

            cx.sync.wait(s["cast"], s["cast"].n)
            cx.sync.wait(s["dve"])
            s["ld"].inc(cx.sync.op("dma_start", out=kstack_dst(xs[:, :], KD, TC),
                                   in_=kstack_src(io.x, 0, KD, TC, c * TC)), 16)

            cx.vector.wait(s["ld"])
            cx.vector.wait(s["pe"])
            cx.vector.wait(s["wst"], s["wst"].n)
            s["dve"].inc(cx.vector.op("tensor_copy", cb[:, :], xs[:, :]))
            cx.scalar.wait(s["ld"])
            cx.scalar.wait(s["pe"])
            cx.scalar.wait(s["wstb"], s["wstb"].n)
            s["actb"].inc(cx.scalar.op("activation", sqb[:, :], xs[:, :], AF.Square))

            emit_ln_stats(cx, cb[:, :], sqb[:, :], KD, TC, consts, pst,
                          rows, bcst, s, DM)
            cx.vector.wait(s["act"])
            cx.vector.wait(s["mm"])
            rb = bcast_free(bcst[:, 0:TC], KD, TC)
            mb = bcast_free(bcst[:, TC:2 * TC], KD, TC)
            s["dve"].inc(cx.vector.op("tensor_mul", tmpf[:, :], xs[:, :], rb))
            cx.vector.wait(s["dve"], s["dve"].n)
            s["dvec"].inc(cx.vector.op("tensor_sub", vln[:, :], tmpf[:, :], mb))

            emit_ffn(cx, vln[:, :], hb[:, :], xs[:, :], x1s[:, :],
                     io.wg_s, io.wu_s, io.wd_s, wmm, wdm, g_t, u_t, ps_m, s, TC,
                     0.5 * S127, wst_target=wst_ffn)

            cx.gpsimd.wait(s["evac"])
            s["st"].inc(cx.gpsimd.op("dma_start",
                                     out=kstack_src(io.x1, 0, KD, TC, c * TC),
                                     in_=kstack_dst(x1s[:, :], KD, TC)), 16)

            cx.vector.wait(s["pe"])
            cx.vector.wait(s["evac"])
            s["dve"].inc(cx.vector.op("tensor_copy", cb[:, :], x1s[:, :]))
            cx.scalar.wait(s["evac"])
            cx.scalar.wait(s["pe"])
            s["actb"].inc(cx.scalar.op("activation", sqb[:, :], x1s[:, :], AF.Square))
            emit_ln_stats(cx, cb[:, :], sqb[:, :], KD, TC, consts, pst,
                          rows, bcst, s, DM)
            cx.vector.wait(s["act"])
            cx.vector.wait(s["st"])
            rb = bcast_free(bcst[:, 0:TC], KD, TC)
            mb = bcast_free(bcst[:, TC:2 * TC], KD, TC)
            s["dve"].inc(cx.vector.op("tensor_mul", tmpf[:, :], x1s[:, :], rb))
            cx.vector.wait(s["dve"], s["dve"].n)
            r = cx.vector.op("tensor_sub", us[:, :], tmpf[:, :], mb)
            s["dvec"].inc(r)
            cx.gpsimd.wait(s["dvec"])
            s["st"].inc(cx.gpsimd.op("dma_start",
                                     out=kstack_src(io.u, 0, KD, TC, c * TC),
                                     in_=kstack_dst(us[:, :], KD, TC)), 16)

            cx.vector.wait(s["pe"])
            cx.vector.wait(s["dvec"])
            s["dve"].inc(cx.vector.op("tensor_copy", cb[:, :], us[:, :]))
            cx.scalar.wait(s["dve"])
            cx.scalar.wait(s["dvec"])
            cx.scalar.wait(s["pe"])
            s["actb"].inc(cx.scalar.op("activation", sqb[:, :], us[:, :], AF.Square))
            emit_ln_stats(cx, cb[:, :], sqb[:, :], KD, TC, consts, pst,
                          rows, bcst, s, DM)
            cx.vector.wait(s["act"])
            rb = bcast_free(bcst[:, 0:TC], KD, TC)
            mb = bcast_free(bcst[:, TC:2 * TC], KD, TC)
            s["dve"].inc(cx.vector.op("tensor_mul", tmpf[:, :], us[:, :], rb))
            cx.vector.wait(s["dve"], s["dve"].n)
            s["dvec"].inc(cx.vector.op("tensor_sub", uns[:, :], tmpf[:, :], mb))
            cx.vector.wait(s["dvec"], s["dvec"].n)
            for d, (wc, bc) in enumerate(((lnw_f, lnb_f), (lnw_b, lnb_b))):
                cx.vector.wait(s["st"])
                for k in range(KD):
                    r = cx.vector.op("tensor_scalar",
                                     vaf[:, k * TC:(k + 1) * TC],
                                     uns[:, k * TC:(k + 1) * TC],
                                     wc(k), bc(k), OP.mult, OP.add)
                s["dve"].inc(r)
                cx.gpsimd.wait(s["dve"])
                s["st"].inc(cx.gpsimd.op("dma_start",
                                         out=kstack_src(io.v[d] if not io.v_chunked
                                                        else io.v[c, d],
                                                        0, KD, TC,
                                                        c * TC if not io.v_chunked else 0),
                                         in_=kstack_dst(vaf[:, :], KD, TC)), 16)
            if v_store_hook is not None:
                v_store_hook(c)

        cx.gpsimd.wait(s["st"])




def p1_sems(cx, pre):
    return {k: cx.sem(pre + k) for k in
            ("ld", "wld", "wldb", "cast", "wst", "wstb", "w", "wb", "pe",
             "mm", "act", "actb", "dve", "dveb", "dvec", "evac", "st")}


def build_p1():
    nc = bass.Bass("TRN2", target_bir_lowering=False)
    TC = 512
    NCH = TOK // TC

    x_in = nc.dram_tensor("x", [DM, TOK], F32, kind="ExternalInput").ap()
    wg_in = nc.dram_tensor("wgT", [DM, HID], I32, kind="ExternalInput").ap()
    wu_in = nc.dram_tensor("wuT", [DM, HID], I32, kind="ExternalInput").ap()
    wd_in = nc.dram_tensor("wdT", [HID, DM], I32, kind="ExternalInput").ap()
    cols_in = nc.dram_tensor("cols", [DM, 4], F32, kind="ExternalInput").ap()

    x1_out = nc.dram_tensor("x1", [DM, TOK], F32, kind="ExternalOutput").ap()
    u_out = nc.dram_tensor("u", [DM, TOK], F32, kind="ExternalOutput").ap()
    v_out = nc.dram_tensor("v", [2, DM, TOK], BF16, kind="ExternalOutput").ap()

    wg_s = nc.dram_tensor("wg_s", [DM, HID], BF16, kind="Internal").ap()
    wu_s = nc.dram_tensor("wu_s", [DM, HID], BF16, kind="Internal").ap()
    wd_s = nc.dram_tensor("wd_s", [HID, DM], BF16, kind="Internal").ap()

    ones_col = nc.inline_tensor(np.ones((P, 1), dtype=BF), "ones_col").ap()
    ones_row = nc.inline_tensor(np.ones((1, P), dtype=np.float32), "ones_rowf").ap()
    eps_col = nc.inline_tensor(np.full((P, 1), EPS, np.float32), "eps_col").ap()

    import types
    io = types.SimpleNamespace(
        x=x_in, wgT=wg_in, wuT=wu_in, wdT=wd_in, cols=cols_in,
        x1=x1_out, u=u_out, v=v_out, v_chunked=False,
        wg_s=wg_s, wu_s=wu_s, wd_s=wd_s,
        ones_col=ones_col, ones_row=ones_row, eps_col=eps_col)
    with contextlib.ExitStack() as es:
        cx = Ctx(nc, es)
        s = p1_sems(cx, "p1")
        with contextlib.ExitStack() as es2:
            emit_p1(nc, cx, io, s, es2)
        cx.flush()
    return nc



TC2 = 512
NCH2 = L // TC2
NT = DQ // P
HALO = 8


def emit_p2(nc, cx, io, s, es, vsrc_all, fused, pre=""):
    sb = lambda n, sh, dt: es.enter_context(nc.sbuf_tensor(pre + n, sh, dt))
    ps = lambda n, sh: es.enter_context(nc.psum_tensor(pre + n, sh, F32))
    RGROUPS = [[0, 1, 2, 3], [4, 5, 6, 7]]
    if True:
        winb = sb("winb", [P, KD * 1024], BF16)
        vsb = sb("vsb", [P, KD * TC2], BF16)

        xzx = sb("xzx", [P, NT * (TC2 + 2 * HALO)], F32)
        szb = sb("szb", [P, 3 * NT * TC2], F32)
        xpf = sb("xpf", [P, 3 * NT * TC2], F32)
        usc = sb("usc", [P, NT * TC2], BF16)
        dels = sb("dels", [P, 2 * NT * TC2], F32)
        spa = sb("spa", [P, TC2], F32)
        spr = sb("spr", [P, TC2], F32)
        dbls = sb("dbls", [96, 3 * TC2], F32)
        dtb = sb("dtb", [RNK, TC2], BF16)
        bcB = sb("bcB", [P, NST * TC2], BF16)
        bcC = sb("bcC", [P, NST * TC2], BF16)
        hbuf = sb("hbuf", [P, NST * TC2], BF16)
        dxb = sb("dxb", [P, NST * TC2], BF16)
        dA2 = sb("dA2", [P, 2 * TC2], BF16)

        ygt = usc
        cvec = sb("cvec", [P, 2 * NT * 5], F32)
        cwv = sb("cwv", [P, 2 * NT * 4], F32)
        asb = sb("asb", [P, 2 * NT * NST], F32)
        wxpb = sb("wxpb", [P, 2 * NT * 96], BF16)
        wdtb = sb("wdtb", [RNK, 2 * DQ], BF16)
        stt = sb("stt", [P, NT * NST], F32)
        onec = sb("onec", [P, 1], F32)
        tca = sb("tca", [P, TC2], F32)
        xpb = sb("xpb", [P, NT * TC2], BF16)

        stg = dataclasses.replace(bcB[:, :].bitcast(F32), ap=[[4096, P], [1, 4096]])
        spb = sb("spb", [P, TC2], F32)[:, :]
        ytm = tca[:, :]
        if fused:
            wosb = sb("wosb", [P, NT * 1024], BF16)
            poshb = sb("poshb", [P, 4 * TC2], F32)

        ps_x = [ps("ps_x0", [P, TC2]), ps("ps_x1", [P, TC2])]
        ps_dt = [ps("ps_dt0", [P, TC2]), ps("ps_dt1", [P, TC2])]
        ps_d = ps("ps_d", [96, TC2])

        def col(ap, j):
            return ap[:, j:j + 1]

        s["ld"].inc(cx.sync.op("dma_start", out=onec[:, :], in_=io.ones_colf), 16)
        for d in range(2):
            for i in range(NT):
                rs = slice(i * P, (i + 1) * P)
                s["ld"].inc(cx.sync.op("dma_start",
                                       out=cvec[:, (d * NT + i) * 5:(d * NT + i + 1) * 5],
                                       in_=io.cvec_q[d, rs, :]), 16)
                s["ld"].inc(cx.sync.op("dma_start",
                                       out=cwv[:, (d * NT + i) * 4:(d * NT + i + 1) * 4],
                                       in_=io.convw_q[d, rs, :]), 16)
                s["ld"].inc(cx.sync.op("dma_start",
                                       out=asb[:, (d * NT + i) * NST:(d * NT + i + 1) * NST],
                                       in_=io.alog_q[d, rs, :]), 16)
        cx.scalar.wait(s["ld"])
        s["act"].inc(cx.scalar.op("activation",
                                  dels[:, 0:2 * NT * NST],
                                  asb[:, :], AF.Exp))
        cx.vector.wait(s["act"])
        s["dve"].inc(cx.vector.op("tensor_scalar_mul", asb[:, :],
                                  dels[:, 0:2 * NT * NST], -1.0))

        for d in range(2):
            for i in range(NT):
                pb = i % 2
                cx.sync.wait(s["dve"], s["dve"].n)
                s["ld"].inc(cx.sync.op("dma_start", out=stg[:, pb * 1024:pb * 1024 + 96],
                                       in_=io.wxp_q[d, i * P:(i + 1) * P, :]), 16)
                cx.vector.wait(s["ld"])
                s["dve"].inc(cx.vector.op(
                    "tensor_copy", wxpb[:, (d * NT + i) * 96:(d * NT + i + 1) * 96],
                    stg[:, pb * 1024:pb * 1024 + 96]))
        for d in range(2):
            cx.sync.wait(s["dve"], s["dve"].n)
            s["ld"].inc(cx.sync.op("dma_start", out=stg[0:RNK, 0:DQ],
                                   in_=io.wdt_q[d, :, :]), 16)
            cx.vector.wait(s["ld"])
            s["dve"].inc(cx.vector.op("tensor_copy", wdtb[:, d * DQ:(d + 1) * DQ],
                                      stg[0:RNK, 0:DQ]))

        XW = TC2 + 2 * HALO
        arseq = {}
        seq = {}

        def pre_a(d, c, buf):
            tok = slice(c * TC2, (c + 1) * TC2)
            xpfb = xpf[:, buf * NT * TC2:(buf + 1) * NT * TC2]
            szbb = szb[:, buf * NT * TC2:(buf + 1) * NT * TC2]

            if fused:
                cx.sync.wait(io.ag_sem, io.ag_target(c))
            prevxz = [k for k in seq if k[0] == "xz"]
            cx.sync.wait(s["pe"], seq[prevxz[-1]] if prevxz else 0)
            s["ld"].inc(cx.sync.op("dma_start", out=kstack_dst(vsb[:, :], KD, TC2),
                                   in_=vsrc_all(d, c)), 16)

            cx.tensor.wait(s["ld"])
            prevp = [k for k in seq if k[0] == "pcp"]
            cx.tensor.wait(s["dve"], seq[prevp[-1]] if prevp else 0)
            for m in range(8):
                mp = m % 2
                cx.tensor.wait(s["evac"], max(0, s["evac"].n - 1))
                for k in range(KD):
                    r = cx.tensor.op("matmul", ps_x[mp][:, :],
                                     winb[:, k * 1024 + m * P:k * 1024 + (m + 1) * P],
                                     vsb[:, k * TC2:(k + 1) * TC2],
                                     start=(k == 0), stop=(k == KD - 1))
                s["pe"].inc(r)
                if m == 7:
                    seq[("xz", d, c)] = s["pe"].n
                i = m % 4
                cx.scalar.wait(s["pe"])
                if m < 4:
                    cx.scalar.wait(s["halo"])
                    cx.scalar.wait(s["dve"])
                    r = cx.scalar.op("activation",
                                     xzx[:, i * XW + HALO:i * XW + HALO + TC2],
                                     ps_x[mp][:, :], AF.Identity,
                                     bias=col(cvec, (d * NT + i) * 5 + 0))
                else:
                    cx.scalar.wait(s["dve"])
                    r = cx.scalar.op("activation", szbb[:, i * TC2:(i + 1) * TC2],
                                     ps_x[mp][:, :], AF.Silu,
                                     bias=col(cvec, (d * NT + i) * 5 + 1))
                s["evac"].inc(r)

        def pre_b(d, c, buf):
            xpfb = xpf[:, buf * NT * TC2:(buf + 1) * NT * TC2]
            for i in range(NT):
                base = i * XW
                dstart = base + HALO
                cx.vector.wait(s["evac"])
                cx.vector.wait(s["act"])
                if d == 0:
                    offs = [dstart - 3 + j for j in range(4)]
                else:
                    offs = [dstart + 3 - j for j in range(4)]
                wj = [col(cwv, (d * NT + i) * 4 + j) for j in range(4)]
                s["dve"].inc(cx.vector.op("tensor_scalar_mul", tca[:, :],
                             xzx[:, offs[0]:offs[0] + TC2], wj[0]))
                for j in range(1, 4):
                    cx.vector.wait(s["dve"], s["dve"].n)
                    s["dve"].inc(cx.vector.op(
                        "scalar_tensor_tensor", tca[:, :],
                        xzx[:, offs[j]:offs[j] + TC2], wj[j],
                        tca[:, :], OP.mult, OP.add))
                if d == 0:
                    hsrc = xzx[:, base + HALO + TC2 - 3:base + HALO + TC2]
                    hdst = xzx[:, base + HALO - 3:base + HALO]
                else:
                    hsrc = xzx[:, base + HALO:base + HALO + 3]
                    hdst = xzx[:, base + HALO + TC2:base + HALO + TC2 + 3]
                s["halo"].inc(cx.vector.op("tensor_copy", hdst, hsrc))
                cx.scalar.wait(s["dve"])
                r = cx.scalar.op("activation", xpfb[:, i * TC2:(i + 1) * TC2],
                                 tca[:, :], AF.Silu,
                                 bias=col(cvec, (d * NT + i) * 5 + 2))
                s["act"].inc(r)

            prevxp = [k for k in seq if k[0] == "xproj"]
            cx.scalar.wait(s["pe"], seq[prevxp[-1]] if prevxp else 0)
            cx.scalar.wait(s["act"], s["act"].n)
            s["evac"].inc(cx.scalar.op("activation", xpb[:, :], xpfb, AF.Copy))

            cx.tensor.wait(s["evac"])
            cx.tensor.wait(s["gst"])
            for i in range(NT):
                r = cx.tensor.op("matmul", ps_d[:, :],
                                 wxpb[:, (d * NT + i) * 96:(d * NT + i + 1) * 96],
                                 xpb[:, i * TC2:(i + 1) * TC2],
                                 start=(i == 0), stop=(i == NT - 1))
            s["pe"].inc(r)
            seq[("xproj", d, c)] = s["pe"].n
            dbb = dbls[:, buf * TC2:(buf + 1) * TC2]
            cx.scalar.wait(s["pe"])
            cx.scalar.wait(s["gst"])
            cx.scalar.wait(s["dve"])
            s["act"].inc(cx.scalar.op("activation", dbb, ps_d[:, :], AF.Copy))
            cx.gpsimd.wait(s["act"])
            s["gst"].inc(cx.gpsimd.op("dma_start", out=io.dbl_i[d, c], in_=dbb), 16)
            cx.gpsimd.wait(s["gst"])
            s["coll"].inc(cx.gpsimd.op("collective_compute", "AllReduce", OP.add,
                                       RGROUPS, [io.dbl_i[d, c]], [io.dbl_o[d, c]]), 1)
            arseq[(d, c)] = s["coll"].n

        dA2_rd = {}
        plseq = {}
        bcdseq = {}
        dirbase = {}

        def consume_head(d, c, buf, hb):
            xpfb = xpf[:, buf * NT * TC2:(buf + 1) * NT * TC2]
            dbb = dbls[:, buf * TC2:(buf + 1) * TC2]
            dlh = dels[:, hb * NT * TC2:(hb + 1) * NT * TC2]
            dth = dtb[:, :]

            cx.gpsimd.wait(s["coll"], arseq[(d, c)])
            cx.gpsimd.wait(s["evac"])
            s["gst"].inc(cx.gpsimd.op("dma_start", out=dbb,
                                      in_=io.dbl_o[d, c]), 16)
            cx.gpsimd.wait(s["act"])
            cx.gpsimd.wait(s["dve"])

            cx.scalar.wait(s["gst"])
            prevdl = [k for k in seq if k[0] == "delta"]
            cx.scalar.wait(s["pe"], seq[prevdl[-1]] if prevdl else 0)
            s["evac"].inc(cx.scalar.op("activation", dth,
                                       dbb[0:RNK, :], AF.Copy))

            cx.tensor.wait(s["evac"])
            for mt in range(NT):
                mp = mt % 2
                cx.tensor.wait(s["evac"], max(0, s["evac"].n - 1))
                r = cx.tensor.op("matmul", ps_dt[mp][:, :],
                                 wdtb[:, d * DQ + mt * P:d * DQ + (mt + 1) * P],
                                 dth, start=True, stop=True)
                s["pe"].inc(r)
                bcol = col(cvec, (d * NT + mt) * 5 + 3)
                cx.scalar.wait(s["pe"])
                cx.scalar.wait(s["pl"])
                cx.scalar.op("activation", spr[:, :], ps_dt[mp][:, :],
                             AF.Relu, bias=bcol)
                s["act"].inc(cx.scalar.op("activation", spa[:, :],
                                          ps_dt[mp][:, :], AF.Abs, bias=bcol))
                cx.scalar.wait(s["act"], s["act"].n)
                s["act"].inc(cx.scalar.op("activation", spb, spa[:, :],
                                          AF.Exp, scale=-1.0))
                cx.scalar.wait(s["act"], s["act"].n)
                r = cx.scalar.op("activation", spa[:, :], spb, AF.Ln,
                                 bias=onec[:, 0:1])
                s["evac"].inc(r)
                cx.gpsimd.wait(s["evac"])
                s["pl"].inc(cx.gpsimd.op("tensor_add",
                                         dlh[:, mt * TC2:(mt + 1) * TC2],
                                         spr[:, :], spa[:, :]))
            plseq[(d, c)] = s["pl"].n
            seq[("delta", d, c)] = s["pe"].n

        def consume_bc(d, c, hb):
            base = io.dbl_o[d, c]
            prevdx = [k for k in seq if k[0] == "dxb3"]
            cx.gpsimd.wait(s["dve"], max(
                seq[prevdx[-1]] if prevdx else 0, dirbase.get(d, 0)))
            sB = dataclasses.replace(
                base, offset=base.offset + 64 * TC2,
                ap=[[0, P], [TC2, NST], [1, TC2]])
            dB = dataclasses.replace(
                bcB[:, :], ap=[list(bcB[:, :].ap[0]), [TC2, NST], [1, TC2]])
            s["bcd"].inc(cx.gpsimd.op("dma_start", out=dB, in_=sB), 16)
            prevch = [k for k in seq if k[0] == "ch3"]
            cx.gpsimd.wait(s["dve"], max(
                seq[prevch[-1]] if prevch else 0, dirbase.get(d, 0)))
            sC = dataclasses.replace(
                base, offset=base.offset + 80 * TC2,
                ap=[[0, P], [TC2, NST], [1, TC2]])
            dC = dataclasses.replace(
                bcC[:, :], ap=[list(bcC[:, :].ap[0]), [TC2, NST], [1, TC2]])
            s["bcd"].inc(cx.gpsimd.op("dma_start", out=dC, in_=sC), 16)
            bcdseq[(d, c)] = s["bcd"].n

        def consume_scans(d, c, buf, hb):
            xpfb = xpf[:, buf * NT * TC2:(buf + 1) * NT * TC2]
            szbb = szb[:, buf * NT * TC2:(buf + 1) * NT * TC2]
            dlh = dels[:, hb * NT * TC2:(hb + 1) * NT * TC2]
            uch = usc[:, :]

            cx.vector.wait(s["yst"])
            if fused:
                prevw = [k for k in seq if k[0] == "wout"]
                cx.vector.wait(s["pe"], seq[prevw[-1]] if prevw else 0)
            cx.vector.wait(s["act"])
            cx.vector.wait(s["pl"], plseq[(d, c)])
            cx.scalar.wait(s["pl"], plseq[(d, c)])
            s["dve"].inc(cx.vector.op("tensor_mul", uch, dlh, xpfb))
            cx.vector.wait(s["dve"], s["dve"].n)

            for i in range(NT):
                if i == 0:
                    cx.vector.wait(s["bcd"], bcdseq[(d, c)])
                ui = uch[:, i * TC2:(i + 1) * TC2]
                ui3 = dataclasses.replace(
                    ui, ap=[list(ui.ap[0]), [0, NST], [1, TC2]])
                b3 = dataclasses.replace(
                    bcB[:, :], ap=[list(bcB[:, :].ap[0]), [TC2, NST], [1, TC2]])
                d3 = dataclasses.replace(
                    dxb[:, :], ap=[list(dxb[:, :].ap[0]), [TC2, NST], [1, TC2]])
                r = cx.vector.op("tensor_mul", d3, ui3, b3)
                s["dve"].inc(r)
                cx.vector.wait(s["dve"], s["dve"].n)
                if i == NT - 1:
                    seq[("dxb3", d, c)] = s["dve"].n
                for n in range(NST):
                    np_ = n % 2
                    cx.scalar.wait(s["dve"], dA2_rd.get(np_, 0))
                    acol = col(asb, (d * NT + i) * NST + n)
                    r = cx.scalar.op("activation",
                                     dA2[:, np_ * TC2:(np_ + 1) * TC2],
                                     dlh[:, i * TC2:(i + 1) * TC2], AF.Exp,
                                     scale=acol)
                    s["act"].inc(r)
                    cx.vector.wait(s["act"])
                    da = dA2[:, np_ * TC2:(np_ + 1) * TC2]
                    dx = dxb[:, n * TC2:(n + 1) * TC2]
                    if d == 1:
                        da = rev(da)
                        dx = rev(dx)
                    r = cx.vector.op("tensor_tensor_scan",
                                     hbuf[:, n * TC2:(n + 1) * TC2], da, dx,
                                     stt[:, i * NST + n:i * NST + n + 1],
                                     OP.mult, OP.add)
                    s["dve"].inc(r)
                    dA2_rd[np_] = s["dve"].n
                hl = dataclasses.replace(
                    hbuf[:, :], offset=hbuf[:, :].offset + TC2 - 1,
                    ap=[list(hbuf[:, :].ap[0]), [TC2, NST]])
                cx.scalar.wait(s["dve"])
                s["act"].inc(cx.scalar.op("activation",
                                          stt[:, i * NST:(i + 1) * NST], hl,
                                          AF.Copy))
                cc = bcC[:, :]
                if d == 1:
                    cc = dataclasses.replace(
                        cc, offset=cc.offset + TC2 - 1,
                        ap=[list(cc.ap[0]), [TC2, NST], [-1, TC2]])
                s["dve"].inc(cx.vector.op("tensor_mul", dxb[:, :],
                                          hbuf[:, :], cc))
                if i == NT - 1:
                    seq[("ch3", d, c)] = s["dve"].n
                cx.vector.wait(s["dve"], s["dve"].n)
                for lw in (8, 4, 2, 1):
                    s["dve"].inc(cx.vector.op(
                        "tensor_add", dxb[:, 0:lw * TC2],
                        dxb[:, 0:lw * TC2], dxb[:, lw * TC2:2 * lw * TC2]))
                    cx.vector.wait(s["dve"], s["dve"].n)
                ysv = dxb[:, 0:TC2]
                if d == 1:
                    ysv = rev(ysv)
                s["dve"].inc(cx.vector.op(
                    "scalar_tensor_tensor", ytm[:, :],
                    xpfb[:, i * TC2:(i + 1) * TC2],
                    col(cvec, (d * NT + i) * 5 + 4), ysv,
                    OP.mult, OP.add))
                cx.vector.wait(s["dve"], s["dve"].n)
                r = cx.vector.op("tensor_mul", ygt[:, i * TC2:(i + 1) * TC2],
                                 ytm[:, :], szbb[:, i * TC2:(i + 1) * TC2])
                s["dve"].inc(r)

            if not fused:
                cx.gpsimd.wait(s["dve"])
                s["yst"].inc(cx.gpsimd.op(
                    "dma_start",
                    out=kstack_src(io.y[d], 0, NT, TC2, c * TC2),
                    in_=kstack_dst(ygt[:, :], NT, TC2)), 16)
            else:
                cx.tensor.wait(s["dve"])
                cx.tensor.wait(s["evac"])
                ppdma = {}
                for m in range(KD):
                    mp = m % 2
                    pp = (m // 2) % 2
                    cx.tensor.wait(s["dve"], max(0, s["dve"].n - 1))
                    for k in range(NT):
                        r = cx.tensor.op(
                            "matmul", ps_x[mp][:, :],
                            wosb[:, k * 1024 + m * P:k * 1024 + (m + 1) * P],
                            ygt[:, k * TC2:(k + 1) * TC2],
                            start=(k == 0), stop=(k == NT - 1))
                    s["pe"].inc(r)
                    wtgt = s["pe"].n
                    if m == KD - 1:
                        seq[("wout", d, c)] = wtgt
                    psem = s["post"] if pp == 0 else s["postb"]
                    cx.vector.wait(s["pe"], wtgt)
                    if m == 0:
                        cx.vector.wait(s["post"])
                        cx.vector.wait(s["postb"])
                    elif m % 2 == 0 and pp in ppdma:
                        cx.vector.wait(psem, ppdma[pp])
                    s["dve"].inc(cx.vector.op(
                        "tensor_copy",
                        poshb[:, (2 * pp + m % 2) * TC2:
                              (2 * pp + m % 2 + 1) * TC2],
                        ps_x[mp][:, :]))
                    if m % 2 == 1:
                        cx.gpsimd.wait(s["dve"], s["dve"].n)
                        if d == 0:
                            pdst = kstack_src(io.pout[0][c // 4],
                                              (m - 1) * P, 2,
                                              TC2, (c % 4) * TC2)
                        else:
                            pdst = kstack_src(io.pout[1][c % 4, c // 4],
                                              (m - 1) * P, 2, TC2, 0)
                        psem.inc(cx.gpsimd.op(
                            "dma_start", out=pdst,
                            in_=kstack_dst(
                                poshb[:, 2 * pp * TC2:(2 * pp + 2) * TC2],
                                2, TC2)), 16)
                        ppdma[pp] = psem.n
                seq[("pcp", d, c)] = s["dve"].n


        for d in range(2):
            rorder = list(range(NCH2)) if d == 0 else list(range(NCH2 - 1, -1, -1))
            for k in range(KD):
                pb = k % 2
                cx.sync.wait(s["dve"], s["dve"].n)
                cx.sync.wait(s["bcd"], s["bcd"].n)
                cx.sync.wait(s["pe"], s["pe"].n)
                s["ld"].inc(cx.sync.op("dma_start",
                                       out=stg[:, pb * 1024:(pb + 1) * 1024],
                                       in_=io.win_q[d, k * P:(k + 1) * P, :]), 16)
                cx.vector.wait(s["ld"])
                cx.vector.wait(s["evac"])
                s["dve"].inc(cx.vector.op(
                    "tensor_copy", winb[:, k * 1024:(k + 1) * 1024],
                    stg[:, pb * 1024:(pb + 1) * 1024]))
            if fused:
                for k in range(NT):
                    pb = k % 2
                    cx.sync.wait(s["dve"], s["dve"].n)
                    s["ld"].inc(cx.sync.op("dma_start",
                                           out=stg[:, pb * 1024:(pb + 1) * 1024],
                                           in_=io.wout_q[d, k * P:(k + 1) * P, :]), 16)
                    cx.vector.wait(s["ld"])
                    s["dve"].inc(cx.vector.op(
                        "tensor_copy", wosb[:, k * 1024:(k + 1) * 1024],
                        stg[:, pb * 1024:(pb + 1) * 1024]))

            cx.tensor.wait(s["dve"])
            cx.tensor.wait(s["ld"])
            cx.scalar.wait(s["dve"])
            cx.scalar.wait(s["ld"])
            cx.vector.wait(s["dve"])
            cx.vector.wait(s["yst"])
            cx.vector.wait(s["post"])
            s["dve"].inc(cx.vector.op("memset", stt[:, :], 0.0))
            for i in range(NT):
                if d == 0:
                    h = cx.vector.op("memset", xzx[:, i * XW:i * XW + HALO], 0.0)
                else:
                    h = cx.vector.op("memset", xzx[:, i * XW + HALO + TC2:(i + 1) * XW], 0.0)
                s["halo"].inc(h)
            dirbase[d] = s["dve"].n

            pre_a(d, rorder[0], 0)
            pre_b(d, rorder[0], 0)
            pre_a(d, rorder[1], 1)
            pre_b(d, rorder[1], 1)
            consume_head(d, rorder[0], 0, 0)
            for idx, c in enumerate(rorder):
                consume_bc(d, c, idx % 2)
                if idx + 2 < len(rorder):
                    pre_a(d, rorder[idx + 2], (idx + 2) % 3)
                consume_scans(d, c, idx % 3, idx % 2)
                if idx + 1 < len(rorder):
                    consume_head(d, rorder[idx + 1], (idx + 1) % 3,
                                 (idx + 1) % 2)
                if idx + 2 < len(rorder):
                    pre_b(d, rorder[idx + 2], (idx + 2) % 3)
            if fused and io.dir_done_hook is not None:
                io.dir_done_hook(d)

        cx.gpsimd.wait(s["yst"])
        cx.gpsimd.wait(s["post"])
        cx.gpsimd.wait(s["postb"])

def p2_sems(cx, pre):
    return {k: cx.sem(pre + k) for k in
            ("ld", "pe", "act", "dve", "evac", "coll", "gst", "yst", "halo",
             "post", "postb", "pl", "bcd")}


def build_p2():
    nc = bass.Bass("TRN2", target_bir_lowering=False)

    v_in = nc.dram_tensor("vfull", [2, DM, L], BF16, kind="ExternalInput").ap()
    win_in = nc.dram_tensor("win_q", [2, DM, 2 * DQ], F32, kind="ExternalInput").ap()
    wxp_in = nc.dram_tensor("wxp_q", [2, DQ, 96], F32, kind="ExternalInput").ap()
    wdt_in = nc.dram_tensor("wdt_q", [2, RNK, DQ], F32, kind="ExternalInput").ap()
    cvec_in = nc.dram_tensor("cvec_q", [2, DQ, 5], F32, kind="ExternalInput").ap()
    convw_in = nc.dram_tensor("convw_q", [2, DQ, 4], F32, kind="ExternalInput").ap()
    alog_in = nc.dram_tensor("alog_q", [2, DQ, NST], F32, kind="ExternalInput").ap()

    y_out = nc.dram_tensor("y", [2, DQ, L], BF16, kind="ExternalOutput").ap()
    dbg_xp = nc.dram_tensor("dbg_xp", [2, DQ, L], F32, kind="ExternalOutput").ap()
    dbg_dl = nc.dram_tensor("dbg_dl", [2, DQ, L], F32, kind="ExternalOutput").ap()
    dbg_st = nc.dram_tensor("dbg_st", [2, NCH2, P, NT * NST], F32, kind="ExternalOutput").ap()

    dbl_i = nc.dram_tensor("dbl_i", [2, NCH2, 96, TC2], F32, kind="Internal").ap()
    dbl_o = nc.dram_tensor("dbl_o", [2, NCH2, 96, TC2], F32, kind="Internal").ap()

    selmat = np.zeros((32, 32 * P), dtype=BF)
    for n in range(32):
        selmat[n, n * P:(n + 1) * P] = 1.0
    sel_c = nc.inline_tensor(selmat, "selmat").ap()
    ones_colf = nc.inline_tensor(np.ones((P, 1), np.float32), "ones_colf2").ap()

    RG = [[0, 1, 2, 3], [4, 5, 6, 7]]

    import types
    io = types.SimpleNamespace(
        win_q=win_in, wxp_q=wxp_in, wdt_q=wdt_in, cvec_q=cvec_in,
        convw_q=convw_in, alog_q=alog_in, y=y_out, dbg_xp=dbg_xp,
        dbg_dl=dbg_dl, dbg_st=dbg_st, dbl_i=dbl_i, dbl_o=dbl_o,
        sel_c=sel_c, ones_colf=ones_colf, wout_q=None, pout=None, ag_sem=None)
    with contextlib.ExitStack() as es:
        cx = Ctx(nc, es)
        s = p2_sems(cx, "p2")
        with contextlib.ExitStack() as es2:
            def vsrc_all(d, c):
                return kstack_src(v_in[d], 0, KD, TC2, c * TC2)
            emit_p2(nc, cx, io, s, es2, vsrc_all, fused=False)
        cx.flush()
    return nc


def _p2_inmaps(inputs, v_by_core):
    return _p2_inmaps_impl(inputs, v_by_core)


def _p2_inmaps_impl(inputs, v_by_core):
    W_in = np.asarray(inputs["L0_attn_W_in"], np.float32)
    W_xp = np.asarray(inputs["L0_attn_W_xproj"], np.float32)
    W_dt = np.asarray(inputs["L0_attn_W_dt"], np.float32)
    b_in = np.asarray(inputs["L0_attn_b_in"], np.float32)
    conv_w = np.asarray(inputs["L0_attn_conv_w"], np.float32)
    conv_b = np.asarray(inputs["L0_attn_conv_b"], np.float32)
    b_dt = np.asarray(inputs["L0_attn_b_dt"], np.float32)
    A_log = np.asarray(inputs["L0_attn_A_log"], np.float32)
    Dp = np.asarray(inputs["L0_attn_D"], np.float32)

    if v_by_core is not None:
        vfull = [np.concatenate([v_by_core[b * 4 + q] for q in range(4)], axis=2)
                 for b in range(2)]

    maps = []
    for c in range(8):
        b, q = c // 4, c % 4
        ds = slice(q * DQ, (q + 1) * DQ)
        win_q = np.concatenate([W_in[:, :, ds], W_in[:, :, DI:][:, :, ds]], axis=2)
        cvec = np.stack([b_in[:, ds], b_in[:, DI:][:, ds], conv_b[:, ds],
                         b_dt[:, ds], Dp[:, ds]], axis=2)
        m = {
            "win_q": np.ascontiguousarray(win_q),
            "wxp_q": np.ascontiguousarray(W_xp[:, ds, :]),
            "wdt_q": np.ascontiguousarray(W_dt[:, :, ds]),
            "cvec_q": np.ascontiguousarray(cvec),
            "convw_q": np.ascontiguousarray(conv_w[:, ds, :]),
            "alog_q": np.ascontiguousarray(A_log[:, ds, :]),
        }
        if v_by_core is not None:
            m["vfull"] = vfull[b]
        maps.append(m)
    return maps


def run_p2(inputs, v_by_core):
    nc = build_p2()
    maps = _p2_inmaps(inputs, v_by_core)
    res = run_bass_kernel_spmd(nc, maps, core_ids=list(range(8)))
    return res.results



def emit_p3(nc, cx, io, s, es, fused, pre=""):
    TC = 512
    NCH = TOK // TC
    KI = DI // P
    sb = lambda n, sh, dt: es.enter_context(nc.sbuf_tensor(pre + n, sh, dt))
    ps = lambda n, sh: es.enter_context(nc.psum_tensor(pre + n, sh, F32))
    if True:
        ys = sb("ys", [P, KI * TC], BF16)
        ysf = ys[:, :].bitcast(F32)
        fwd_s = sb("fwd_s", [P, KD * TC], F32)
        bwd_s = sb("bwd_s", [P, KD * TC], F32)
        comb = sb("comb", [P, 2 * KD * TC], BF16)
        g_c = sb("g_c", [P, 2 * TC], F32)
        v_c = sb("v_c", [P, 2 * TC], F32)
        x1s = sb("x1s", [P, KD * TC], F32)
        us = sb("us", [P, KD * TC], F32)
        x2s = sb("x2s", [P, KD * TC], F32)
        cb = sb("cb", [P, KD * TC], BF16)
        sqb = sb("sqb", [P, KD * TC], BF16)
        vln = sb("vln", [P, KD * TC], BF16)
        hb = sb("hb", [P, MH * TC], BF16)
        wmm = sb("wmm", [P, 4 * KD * P], BF16)
        wdm = sb("wdm", [P, 2 * MH * P], BF16)
        rows = sb("rows", [1, 4 * TC], F32)
        bcst = sb("bcst", [P, 2 * TC], F32)
        colv = sb("colv", [P, KD * 4], F32)
        g_t = sb("g_t", [P, 2 * TC], BF16)
        u_t = sb("u_t", [P, 2 * TC], BF16)
        onc = sb("onc", [P, 1], BF16)
        onr = sb("onr", [1, P], F32)
        epc = sb("epc", [P, 1], F32)
        os_ = x1s
        hbf32 = hb[:, :].bitcast(F32)
        tmpf = dataclasses.replace(
            hbf32, ap=[[list(hbf32.ap[0])[0], P], [1, KD * TC]])

        ps_s = ps("ps_s", [1, TC])
        ps_q = ps("ps_q", [1, TC])
        ps_b = ps("ps_b", [P, 2 * TC])
        ps_m = [ps("ps_m0", [P, TC]), ps("ps_m1", [P, TC]),
                ps("ps_m2", [P, TC]), ps("ps_m3", [P, TC])]


        s["ld"].inc(cx.sync.op("dma_start", out=onc[:, :], in_=io.ones_col3), 16)
        s["ld"].inc(cx.sync.op("dma_start", out=onr[:, :], in_=io.ones_row3), 16)
        s["ld"].inc(cx.sync.op("dma_start", out=epc[:, :], in_=io.eps_col3), 16)
        colsrc = dataclasses.replace(io.cols3, ap=[[4, P], [4 * P, KD], [1, 4]])
        s["ld"].inc(cx.sync.op("dma_start", out=colv[:, :], in_=colsrc), 16)

        if fused:
            conv_tiles = []
        else:
            conv_tiles = (
                [(io.wgT, io.wg_s, k, HID) for k in range(KD)]
                + [(io.wuT, io.wu_s, k, HID) for k in range(KD)]
                + [(io.wdT, io.wd_s, k, DM) for k in range(MH)]
                + [(io.woutT, io.wo_s, k, DM) for k in range(2 * KI)]
                + [(io.wgT2, io.wg2_s, k, DM) for k in range(KI)]
                + [(io.wvT2, io.wv2_s, k, DM) for k in range(KI)]
            )
        emit_weight_convert(cx, conv_tiles,
                            [x1s[:, :].bitcast(I32), us[:, :].bitcast(I32)],
                            [cb[:, :], sqb[:, :]],
                            [s["wld"], s["wldb"]], s["cast"],
                            [s["wst"], s["wstb"]])

        def lncol(j):
            return lambda k: colv[:, 4 * k + j:4 * k + j + 1]
        bout_f, bout_b, bg_c, bv_c = lncol(0), lncol(1), lncol(2), lncol(3)
        consts = (onc[:, :], onr[:, :], epc[0:1, :])
        pst = (ps_s, ps_q, ps_b)

        corder = list(reversed(range(NCH))) if fused else list(range(NCH))
        for c in corder:
            col = slice(c * TC, (c + 1) * TC)

            cx.sync.wait(s["cast"])
            cx.sync.wait(s["dve"])
            cx.sync.wait(s["dvec"])
            cx.sync.wait(s["st"])
            s["ld"].inc(cx.sync.op("dma_start", out=kstack_dst(x1s[:, :], KD, TC),
                                   in_=kstack_src(io.x1, 0, KD, TC, c * TC)), 16)
            s["ld"].inc(cx.sync.op("dma_start", out=kstack_dst(us[:, :], KD, TC),
                                   in_=kstack_src(io.u, 0, KD, TC, c * TC)), 16)

            for d in range(2):
                dst = fwd_s if d == 0 else bwd_s
                bcol = bout_f if d == 0 else bout_b
                if fused:
                    cx.sync.wait(io.rs_sem, 1 if d == 0 else 2 + (3 - c))
                    cx.sync.wait(s["evac"])
                    ysrc = (kstack_src(io.pout_red[0], 0, KD, TC, c * TC)
                            if d == 0 else
                            kstack_src(io.pout_red[1][c], 0, KD, TC, 0))
                    s["ld"].inc(cx.sync.op(
                        "dma_start", out=kstack_dst(ysf, KD, TC),
                        in_=ysrc), 16)
                    cx.vector.wait(s["ld"])
                    cx.vector.wait(s["st"])
                    for m in range(KD):
                        r = cx.vector.op("scalar_tensor_tensor",
                                         dst[:, m * TC:(m + 1) * TC],
                                         ysf[:, m * TC:(m + 1) * TC],
                                         bcol(m), us[:, m * TC:(m + 1) * TC],
                                         OP.add, OP.add)
                        s["evac"].inc(r)
                    continue
                cx.sync.wait(s["mm"])
                s["ld"].inc(cx.sync.op("dma_start", out=kstack_dst(ys[:, :], KI, TC),
                                       in_=kstack_src(io.yq[d], 0, KI, TC, c * TC)), 16)
                for m in range(KD):
                    pb = m % 2
                    wsl = wdm[:, pb * MH * P:pb * MH * P + KI * P]
                    cx.sync.wait(s["mm"], max(0, s["mm"].n - 1))
                    cx.sync.wait(s["wst"])
                    sw = s["w"] if pb == 0 else s["wb"]
                    sw.inc(cx.sync.op("dma_start",
                                      out=kstack_dst(wsl, KI, P),
                                      in_=kstack_src(io.wo_s, d * DI, KI, P, m * P)), 16)
                    cx.tensor.wait(sw, sw.n)
                    cx.tensor.wait(s["ld"])
                    cx.tensor.wait(s["evac"], max(0, s["evac"].n - 1))
                    for k in range(KI):
                        r = cx.tensor.op("matmul", ps_m[pb][:, :],
                                         wsl[:, k * P:(k + 1) * P],
                                         ys[:, k * TC:(k + 1) * TC],
                                         start=(k == 0), stop=(k == KI - 1))
                    s["mm"].inc(r)
                    cx.vector.wait(s["mm"])
                    cx.vector.wait(s["st"])
                    r = cx.vector.op("scalar_tensor_tensor",
                                     dst[:, m * TC:(m + 1) * TC], ps_m[pb][:, :],
                                     bcol(m), us[:, m * TC:(m + 1) * TC],
                                     OP.add, OP.add)
                    s["evac"].inc(r)

            cx.vector.wait(s["evac"])
            cx.vector.wait(s["mm"])
            cx.vector.op("tensor_copy", comb[:, 0:KD * TC], fwd_s[:, :])
            s["dvec"].inc(cx.vector.op("tensor_copy", comb[:, KD * TC:], bwd_s[:, :]))

            for m in range(KD):
                pb = m % 2
                wsl = wmm[:, (2 * pb) * KD * P:(2 * pb) * KD * P + 2 * KD * P]
                cx.sync.wait(s["mm"], max(0, s["mm"].n - 1))
                cx.sync.wait(s["wst"])
                sw = s["w"] if pb == 0 else s["wb"]
                sw.inc(cx.sync.op("dma_start", out=kstack_dst(wsl, 2 * KD, P),
                                  in_=kstack_src(io.wg2_s, 0, 2 * KD, P, m * P)), 16)
                wsl2 = wdm[:, pb * MH * P:pb * MH * P + 2 * KD * P]
                cx.sync.wait(s["mm"], max(0, s["mm"].n - 1))
                sw.inc(cx.sync.op("dma_start", out=kstack_dst(wsl2, 2 * KD, P),
                                  in_=kstack_src(io.wv2_s, 0, 2 * KD, P, m * P)), 16)
                cx.tensor.wait(sw, sw.n)
                cx.tensor.wait(s["dvec"])
                cx.tensor.wait(s["evac"], max(0, s["evac"].n - 1))
                for k in range(2 * KD):
                    cx.tensor.op("matmul", ps_m[2 * pb][:, :],
                                 wsl[:, k * P:(k + 1) * P],
                                 comb[:, k * TC:(k + 1) * TC],
                                 start=(k == 0), stop=(k == 2 * KD - 1))
                for k in range(2 * KD):
                    r = cx.tensor.op("matmul", ps_m[2 * pb + 1][:, :],
                                     wsl2[:, k * P:(k + 1) * P],
                                     comb[:, k * TC:(k + 1) * TC],
                                     start=(k == 0), stop=(k == 2 * KD - 1))
                s["mm"].inc(r)
                gc = g_c[:, pb * TC:(pb + 1) * TC]
                vc = v_c[:, pb * TC:(pb + 1) * TC]
                cx.scalar.wait(s["mm"])
                cx.scalar.wait(s["dve"], max(0, s["dve"].n - 1))
                cx.scalar.op("activation", gc, ps_m[2 * pb][:, :],
                             AF.Sigmoid, bias=bg_c(m))
                s["evac"].inc(cx.scalar.op("activation", vc,
                                           ps_m[2 * pb + 1][:, :],
                                           AF.Identity, bias=bv_c(m)))
                mt = slice(m * TC, (m + 1) * TC)
                cx.vector.wait(s["evac"])
                s["dve"].inc(cx.vector.op("tensor_add", x2s[:, mt],
                                          fwd_s[:, mt], bwd_s[:, mt]))
                cx.vector.wait(s["dve"], s["dve"].n)
                s["dve"].inc(cx.vector.op("tensor_sub", vc, vc, x2s[:, mt]))
                cx.vector.wait(s["dve"], s["dve"].n)
                s["dve"].inc(cx.vector.op("tensor_mul", gc, gc, vc))
                cx.vector.wait(s["dve"], s["dve"].n)
                s["dve"].inc(cx.vector.op("tensor_add", gc, gc, x2s[:, mt]))
                cx.vector.wait(s["dve"], s["dve"].n)
                r = cx.vector.op("scalar_tensor_tensor", x2s[:, mt],
                                 gc, 0.5, x1s[:, mt], OP.mult, OP.add)
                s["dve"].inc(r)

            cx.vector.wait(s["pe"])
            s["dve"].inc(cx.vector.op("tensor_copy", cb[:, :], x2s[:, :]))
            cx.scalar.wait(s["dve"])
            cx.scalar.wait(s["pe"])
            s["actb"].inc(cx.scalar.op("activation", sqb[:, :], x2s[:, :], AF.Square))
            emit_ln_stats(cx, cb[:, :], sqb[:, :], KD, TC, consts, pst,
                          rows, bcst, s, DM)
            cx.vector.wait(s["act"])
            cx.vector.wait(s["mm"])
            rb = bcast_free(bcst[:, 0:TC], KD, TC)
            mb = bcast_free(bcst[:, TC:2 * TC], KD, TC)
            s["dve"].inc(cx.vector.op("tensor_mul", tmpf, x2s[:, :], rb))
            cx.vector.wait(s["dve"], s["dve"].n)
            s["dvec"].inc(cx.vector.op("tensor_sub", vln[:, :], tmpf, mb))

            emit_ffn(cx, vln[:, :], hb[:, :], x2s[:, :], os_[:, :],
                     io.wg_s, io.wu_s, io.wd_s, wmm, wdm, g_t, u_t, ps_m, s, TC,
                     0.5 * S127)

            cx.gpsimd.wait(s["evac"])
            s["st"].inc(cx.gpsimd.op("dma_start",
                                     out=kstack_src(io.o, 0, KD, TC, c * TC),
                                     in_=kstack_dst(os_[:, :], KD, TC)), 16)

        cx.gpsimd.wait(s["st"])




def build_p3():
    nc = bass.Bass("TRN2", target_bir_lowering=False)
    TC = 512
    NCH = TOK // TC
    KI = DI // P

    y_in = nc.dram_tensor("yq", [2, DI, TOK], BF16, kind="ExternalInput").ap()
    x1_in = nc.dram_tensor("x1", [DM, TOK], F32, kind="ExternalInput").ap()
    u_in = nc.dram_tensor("u", [DM, TOK], F32, kind="ExternalInput").ap()
    wout_in = nc.dram_tensor("woutT", [2 * DI, DM], F32, kind="ExternalInput").ap()
    wg2_in = nc.dram_tensor("wgT2", [DI, DM], F32, kind="ExternalInput").ap()
    wv2_in = nc.dram_tensor("wvT2", [DI, DM], F32, kind="ExternalInput").ap()
    wg_in = nc.dram_tensor("wgT", [DM, HID], I32, kind="ExternalInput").ap()
    wu_in = nc.dram_tensor("wuT", [DM, HID], I32, kind="ExternalInput").ap()
    wd_in = nc.dram_tensor("wdT", [HID, DM], I32, kind="ExternalInput").ap()
    cols_in = nc.dram_tensor("cols3", [DM, 4], F32, kind="ExternalInput").ap()

    o_out = nc.dram_tensor("o", [DM, TOK], F32, kind="ExternalOutput").ap()

    wo_s = nc.dram_tensor("wo_s", [2 * DI, DM], BF16, kind="Internal").ap()
    wg2_s = nc.dram_tensor("wg2_s", [DI, DM], BF16, kind="Internal").ap()
    wv2_s = nc.dram_tensor("wv2_s", [DI, DM], BF16, kind="Internal").ap()
    wg_s = nc.dram_tensor("wg_s", [DM, HID], BF16, kind="Internal").ap()
    wu_s = nc.dram_tensor("wu_s", [DM, HID], BF16, kind="Internal").ap()
    wd_s = nc.dram_tensor("wd_s", [HID, DM], BF16, kind="Internal").ap()

    ones_col = nc.inline_tensor(np.ones((P, 1), dtype=BF), "ones_col3").ap()
    ones_row = nc.inline_tensor(np.ones((1, P), np.float32), "ones_row3").ap()
    eps_col = nc.inline_tensor(np.full((P, 1), EPS, np.float32), "eps_col3").ap()

    import types
    io = types.SimpleNamespace(
        yq=y_in, x1=x1_in, u=u_in, woutT=wout_in, wgT2=wg2_in, wvT2=wv2_in,
        wgT=wg_in, wuT=wu_in, wdT=wd_in, cols3=cols_in, o=o_out,
        wo_s=wo_s, wg2_s=wg2_s, wv2_s=wv2_s, wg_s=wg_s, wu_s=wu_s, wd_s=wd_s,
        ones_col3=ones_col, ones_row3=ones_row, eps_col3=eps_col,
        pout_red=None, rs_sem=None)
    with contextlib.ExitStack() as es:
        cx = Ctx(nc, es)
        s = p1_sems(cx, "p3")
        with contextlib.ExitStack() as es2:
            emit_p3(nc, cx, io, s, es2, fused=False)
        cx.flush()
    return nc


def _p3_inmaps(inputs, y_by_core, p1res):
    W_out = np.asarray(inputs["L0_attn_W_out"], np.float32)
    b_out = np.asarray(inputs["L0_attn_b_out"], np.float32)
    Wg = np.asarray(inputs["L0_attn_Wg"], np.float32)
    Wv = np.asarray(inputs["L0_attn_Wv"], np.float32)
    bg = np.asarray(inputs["L0_attn_bg"], np.float32)
    bv = np.asarray(inputs["L0_attn_bv"], np.float32)
    wgT = np.ascontiguousarray(inputs["L0_ffn_w_gate"].T)
    wuT = np.ascontiguousarray(inputs["L0_ffn_w_up"].T)
    wdT = np.ascontiguousarray(inputs["L0_ffn_w_down"].T)
    cols = np.ascontiguousarray(np.stack([b_out[0], b_out[1], bg, bv], axis=1))
    woutT = np.ascontiguousarray(np.concatenate([W_out[0], W_out[1]], axis=0))
    if y_by_core is not None:
        yfull = [np.concatenate([y_by_core[b * 4 + q] for q in range(4)], axis=1)
                 for b in range(2)]
    maps = []
    for c in range(8):
        b, q = c // 4, c % 4
        ts = slice(q * TOK, (q + 1) * TOK)
        m = {
            "wgT2": Wg, "wvT2": Wv,
            "wgT": wgT, "wuT": wuT, "wdT": wdT, "cols3": cols,
        }
        if y_by_core is not None:
            m["yq"] = np.ascontiguousarray(yfull[b][:, :, ts])
            m["x1"] = p1res[c]["x1"]
            m["u"] = p1res[c]["u"]
            m["woutT"] = woutT
        maps.append(m)
    return maps


def run_p3(inputs, y_by_core, p1res):
    nc = build_p3()
    maps = _p3_inmaps(inputs, y_by_core, p1res)
    res = run_bass_kernel_spmd(nc, maps, core_ids=list(range(8)))
    return res.results



def phase_barrier(cx, name, pre_waits=()):
    pb = cx.sem(name)
    for qn in ("sync", "tensor", "scalar", "vector", "gpsimd"):
        q = getattr(cx, qn)
        for sem in pre_waits:
            q.wait(sem)
        pb.inc(q.op("nop"))
    target = pb.n
    for qn in ("sync", "tensor", "scalar", "vector", "gpsimd"):
        getattr(cx, qn).wait(pb, target)


def build_fused():
    nc = bass.Bass("TRN2", target_bir_lowering=False)
    import types

    x_in = nc.dram_tensor("x", [DM, TOK], F32, kind="ExternalInput").ap()
    wg_in = nc.dram_tensor("wgT", [DM, HID], I32, kind="ExternalInput").ap()
    wu_in = nc.dram_tensor("wuT", [DM, HID], I32, kind="ExternalInput").ap()
    wd_in = nc.dram_tensor("wdT", [HID, DM], I32, kind="ExternalInput").ap()
    cols_in = nc.dram_tensor("cols", [DM, 4], F32, kind="ExternalInput").ap()
    win_in = nc.dram_tensor("win_q", [2, DM, 2 * DQ], F32, kind="ExternalInput").ap()
    wxp_in = nc.dram_tensor("wxp_q", [2, DQ, 96], F32, kind="ExternalInput").ap()
    wdt_in = nc.dram_tensor("wdt_q", [2, RNK, DQ], F32, kind="ExternalInput").ap()
    cvec_in = nc.dram_tensor("cvec_q", [2, DQ, 5], F32, kind="ExternalInput").ap()
    convw_in = nc.dram_tensor("convw_q", [2, DQ, 4], F32, kind="ExternalInput").ap()
    alog_in = nc.dram_tensor("alog_q", [2, DQ, NST], F32, kind="ExternalInput").ap()
    wout_in = nc.dram_tensor("wout_q", [2, DQ, DM], F32, kind="ExternalInput").ap()
    wg2_in = nc.dram_tensor("wgT2", [DI, DM], F32, kind="ExternalInput").ap()
    wv2_in = nc.dram_tensor("wvT2", [DI, DM], F32, kind="ExternalInput").ap()
    cols3_in = nc.dram_tensor("cols3", [DM, 4], F32, kind="ExternalInput").ap()

    o_out = nc.dram_tensor("o", [DM, TOK], F32, kind="ExternalOutput").ap()

    wg_s = nc.dram_tensor("wg_s", [DM, HID], BF16, kind="Internal").ap()
    wu_s = nc.dram_tensor("wu_s", [DM, HID], BF16, kind="Internal").ap()
    wd_s = nc.dram_tensor("wd_s", [HID, DM], BF16, kind="Internal").ap()
    wg2_s = nc.dram_tensor("wg2_s", [DI, DM], BF16, kind="Internal").ap()
    wv2_s = nc.dram_tensor("wv2_s", [DI, DM], BF16, kind="Internal").ap()
    x1_sp = nc.dram_tensor("x1_sp", [DM, TOK], F32, kind="Internal").ap()
    u_sp = nc.dram_tensor("u_sp", [DM, TOK], F32, kind="Internal").ap()
    v_mine = nc.dram_tensor("v_mine", [4, 2, DM, TC2], BF16, kind="Internal").ap()
    v_g = nc.dram_tensor("v_g", [4, 4, 2, DM, TC2], BF16, kind="Internal").ap()
    dbl_i = nc.dram_tensor("dbl_i", [2, NCH2, 96, TC2], F32, kind="Internal").ap()
    dbl_o = nc.dram_tensor("dbl_o", [2, NCH2, 96, TC2], F32, kind="Internal").ap()
    pout_f = nc.dram_tensor("pout_f", [4, DM, TOK], F32, kind="Internal").ap()
    pout_b = nc.dram_tensor("pout_b", [4, 4, DM, TC2], F32, kind="Internal").ap()
    pred_f = nc.dram_tensor("pred_f", [DM, TOK], F32, kind="Internal").ap()
    pred_b = nc.dram_tensor("pred_b", [DM, TOK], F32, kind="Internal").ap()

    ones_col = nc.inline_tensor(np.ones((P, 1), dtype=BF), "ones_col").ap()
    ones_row = nc.inline_tensor(np.ones((1, P), np.float32), "ones_rowf").ap()
    eps_col = nc.inline_tensor(np.full((P, 1), EPS, np.float32), "eps_col").ap()
    selmat = np.zeros((32, 32 * P), dtype=BF)
    for n in range(32):
        selmat[n, n * P:(n + 1) * P] = 1.0
    sel_c = nc.inline_tensor(selmat, "selmat").ap()
    ones_colf = nc.inline_tensor(np.ones((P, 1), np.float32), "ones_colf2").ap()

    RG = [[0, 1, 2, 3], [4, 5, 6, 7]]

    with contextlib.ExitStack() as es:
        cx = Ctx(nc, es)

        io1 = types.SimpleNamespace(
            x=x_in, wgT=wg_in, wuT=wu_in, wdT=wd_in, cols=cols_in,
            x1=x1_sp, u=u_sp, v=v_mine, v_chunked=True,
            wg_s=wg_s, wu_s=wu_s, wd_s=wd_s,
            ones_col=ones_col, ones_row=ones_row, eps_col=eps_col)
        s1 = p1_sems(cx, "p1")
        s_ag = cx.sem("ag")
        agseq = {}
        with contextlib.ExitStack() as es1:
            extra = ([(wg2_in, wg2_s, k, DM) for k in range(DI // P)]
                     + [(wv2_in, wv2_s, k, DM) for k in range(DI // P)])
            emit_p1(nc, cx, io1, s1, es1, pre="p1_", extra_tiles=extra)
        for c in range(4):
            vm2 = dataclasses.replace(v_mine[c], ap=[[TC2, 2 * DM], [1, TC2]])
            vg2 = dataclasses.replace(v_g[c], ap=[[TC2, 4 * 2 * DM], [1, TC2]])
            s_ag.inc(cx.gpsimd.op("collective_compute", "AllGather", OP.bypass,
                                  RG, [vm2], [vg2]), 1)
            agseq[c] = s_ag.n
        phase_barrier(cx, "pb1", pre_waits=(s1["ld"], s1["w"], s1["wb"], s1["wld"],
                                    s1["wldb"], s1["wst"],
                                    s1["wstb"]))

        io2 = types.SimpleNamespace(
            win_q=win_in, wxp_q=wxp_in, wdt_q=wdt_in, cvec_q=cvec_in,
            convw_q=convw_in, alog_q=alog_in, y=None, dbg_xp=None,
            dbg_dl=None, dbg_st=None, dbl_i=dbl_i, dbl_o=dbl_o,
            sel_c=sel_c, ones_colf=ones_colf, wout_q=wout_in,
            pout=(pout_f, pout_b), ag_sem=s_ag,
            ag_target=lambda c: agseq[c % 4], dir_done_hook=None)
        s2 = p2_sems(cx, "p2")
        s_rs = cx.sem("rs")

        def dir_hook(d):
            cx.gpsimd.wait(s2["post"])
            cx.gpsimd.wait(s2["postb"])
            pin = pout_f if d == 0 else pout_b
            pot = pred_f if d == 0 else pred_b
            if d == 0:
                pi2 = dataclasses.replace(pin, ap=[[TOK, 4 * DM], [1, TOK]])
                po2 = dataclasses.replace(pot, ap=[[TOK, DM], [1, TOK]])
                s_rs.inc(cx.gpsimd.op("collective_compute", "ReduceScatter",
                                      OP.add, RG, [pi2], [po2]), 1)
            else:
                for j in (3, 2, 1, 0):
                    pi2 = dataclasses.replace(
                        pin[j], ap=[[TC2, 4 * DM], [1, TC2]])
                    po2 = dataclasses.replace(
                        pot[j], ap=[[TC2, DM], [1, TC2]])
                    s_rs.inc(cx.gpsimd.op("collective_compute", "ReduceScatter",
                                          OP.add, RG, [pi2], [po2]), 1)
        io2.dir_done_hook = dir_hook

        with contextlib.ExitStack() as es2:
            def vsrc_all(d, c):
                return kstack_src(v_g[c % 4, c // 4, d], 0, KD, TC2, 0)
            emit_p2(nc, cx, io2, s2, es2, vsrc_all, fused=True, pre="p2_")
        phase_barrier(cx, "pb2", pre_waits=(s2["ld"],))

        io3 = types.SimpleNamespace(
            yq=None, x1=x1_sp, u=u_sp, woutT=None, wgT2=wg2_in, wvT2=wv2_in,
            wgT=wg_in, wuT=wu_in, wdT=wd_in, cols3=cols3_in, o=o_out,
            wo_s=None, wg2_s=wg2_s, wv2_s=wv2_s, wg_s=wg_s, wu_s=wu_s,
            wd_s=wd_s, ones_col3=ones_col, ones_row3=ones_row,
            eps_col3=eps_col, pout_red=(pred_f, pred_b), rs_sem=s_rs)
        s3 = p1_sems(cx, "p3")
        with contextlib.ExitStack() as es3:
            emit_p3(nc, cx, io3, s3, es3, fused=True, pre="p3_")

        cx.flush()
    return nc


def _fused_inmaps(inputs):
    m1 = _p1_inmaps(inputs)
    m2 = _p2_inmaps(inputs, None)
    m3 = _p3_inmaps(inputs, None, None)
    W_out = np.asarray(inputs["L0_attn_W_out"], np.float32)
    maps = []
    for c in range(8):
        b, q = c // 4, c % 4
        ds = slice(q * DQ, (q + 1) * DQ)
        mm = {}
        mm.update(m1[c])
        mm.update({k: v for k, v in m2[c].items() if k != "vfull"})
        mm.update({k: v for k, v in m3[c].items()
                   if k in ("wgT2", "wvT2", "cols3")})
        mm["wout_q"] = np.ascontiguousarray(W_out[:, ds, :])
        maps.append(mm)
    return maps


def kernel(**inputs):
    nc = build_fused()
    maps = _fused_inmaps(inputs)
    res = run_bass_kernel_spmd(nc, maps, core_ids=list(range(8)))
    out = np.empty((B, L, DM), np.float32)
    for c in range(8):
        b, q = c // 4, c % 4
        out[b, q * TOK:(q + 1) * TOK, :] = res.results[c]["o"].T
    return out


def kernel_unfused(**inputs):
    p1 = run_p1(inputs)
    v_by_core = [p1[c]["v"].view(ml_dtypes.bfloat16) for c in range(8)]
    p2 = run_p2(inputs, v_by_core)
    y_by_core = [p2[c]["y"].view(ml_dtypes.bfloat16) for c in range(8)]
    p3 = run_p3(inputs, y_by_core, p1)
    out = np.empty((B, L, DM), np.float32)
    for c in range(8):
        b, q = c // 4, c % 4
        out[b, q * TOK:(q + 1) * TOK, :] = p3[c]["o"].T
    return out



def _p1_inmaps(inputs):
    x = inputs["L0_attn_x"]
    wgT = np.ascontiguousarray(inputs["L0_ffn_w_gate"].T)
    wuT = np.ascontiguousarray(inputs["L0_ffn_w_up"].T)
    wdT = np.ascontiguousarray(inputs["L0_ffn_w_down"].T)
    lnw = np.asarray(inputs["L0_attn_ln_w"], np.float32)
    lnb = np.asarray(inputs["L0_attn_ln_b"], np.float32)
    cols = np.ascontiguousarray(np.stack([lnw[0], lnb[0], lnw[1], lnb[1]], axis=1))
    maps = []
    for c in range(8):
        b, q = c // 4, c % 4
        xs = np.ascontiguousarray(np.asarray(x[b, q * TOK:(q + 1) * TOK, :], np.float32).T)
        maps.append({"x": xs, "wgT": wgT, "wuT": wuT, "wdT": wdT, "cols": cols})
    return maps


def run_p1(inputs):
    nc = build_p1()
    maps = _p1_inmaps(inputs)
    res = run_bass_kernel_spmd(nc, maps, core_ids=list(range(8)))
    return res.results


if __name__ == "__main__":
    nc = build_p1()
    print("P1 built ok")
